# version 02
# baseline (speedup 1.0000x reference)
import math
from contextlib import ExitStack

import numpy as np
import ml_dtypes

import concourse.bass as bass
import concourse.mybir as mybir
from concourse.bass_utils import run_bass_kernel_spmd

F32 = mybir.dt.float32
BF16 = mybir.dt.bfloat16
U32 = mybir.dt.uint32
AF = mybir.ActivationFunctionType
ALU = mybir.AluOpType
AX = mybir.AxisListType.X
BFD = ml_dtypes.bfloat16

NCORES = 8
B, T, C = 2, 4096, 2048
TPC = 8
TILES = [[c, 7 - c, 8 + c, 15 - c, 16 + c, 23 - c, 24 + c, 31 - c]
         for c in range(4)]
NEG = -1e9

ENGS = ("tensor", "vector", "scalar", "gpsimd", "sync")


class Phase:
    def __init__(self):
        self.q = {e: [] for e in ENGS}

    def add(self, eng, fn):
        self.q[eng].append(fn)


def flush(nc, p):
    with nc.Block() as blk:
        for eng in ENGS:
            fns = p.q[eng]
            if not fns:
                continue

            def body(e, fns=fns):
                for f in fns:
                    f(e)

            getattr(blk, eng)(body)


class S:

    def __init__(self, nc, st):
        names = ("dma", "wdma", "pe", "dve", "act")
        self.h = {n: st.enter_context(nc.semaphore(n)) for n in names}
        self.c = {n: 0 for n in names}

    def w(self, P, eng, sem, val):
        if val > 0:
            P.add(eng, lambda e, h=self.h[sem], v=val: e.wait_ge(h, v))

    def op(self, P, eng, sem, build):
        self.c[sem] += 16 if sem in ("dma", "wdma") else 1
        inc = 16 if sem in ("dma", "wdma") else 1
        P.add(eng, lambda e, b=build, h=self.h[sem], i=inc: b(e).then_inc(h, i))
        return self.c[sem]

    def raw(self, P, eng, build):
        P.add(eng, build)

    def selfw(self, P, eng, sem):
        self.w(P, eng, sem, self.c[sem])



def em_load_r(nc, P, s, r_sb, r_dram):
    return s.op(P, "sync", "dma", lambda e: e.dma_start(
        r_sb.ap().rearrange("p (m d) -> p m d", m=TPC),
        r_dram.rearrange("(m p) d -> p m d", p=128)))


def em_store_r(nc, P, s, r_dram, r_sb, waits):
    for sem, val in waits:
        s.w(P, "sync", sem, val)
    v = s.op(P, "sync", "dma", lambda e: e.dma_start(
        r_dram.rearrange("(m p) d -> p m d", p=128),
        r_sb.ap().rearrange("p (m d) -> p m d", m=TPC)))
    s.w(P, "sync", "dma", v)


def em_consts(nc, P, s, st, values):
    n = max(1, len(values))
    cst = st.enter_context(nc.sbuf_tensor("cst", [128, n], F32))
    d = 0
    for i, v in enumerate(values):
        d = s.op(P, "vector", "dve", lambda e, i=i, v=v: e.memset(
            cst[:, i:i + 1], float(v)))
    return cst, d


def em_ln(nc, P, s, st, name, r_sb, xnt, ident, r_rdy, eps_ap):
    sq = st.enter_context(nc.sbuf_tensor(name + "sq", [128, 2048], F32))
    xn = [st.enter_context(nc.sbuf_tensor(name + f"xn{i}", [128, 2048], BF16))
          for i in range(2)]
    sums = st.enter_context(nc.sbuf_tensor(name + "sm", [128, 8], F32))
    sumsq = st.enter_context(nc.sbuf_tensor(name + "sq8", [128, 8], F32))
    mean = st.enter_context(nc.sbuf_tensor(name + "mn", [128, 8], F32))
    var = st.enter_context(nc.sbuf_tensor(name + "vr", [128, 8], F32))
    rstd = st.enter_context(nc.sbuf_tensor(name + "rs", [128, 8], F32))
    tp = [st.enter_context(nc.psum_tensor(name + f"tp{i}", [128, 512], BF16))
          for i in range(2)]

    s.w(P, "scalar", "dma", r_rdy)
    s.w(P, "vector", "dma", r_rdy)
    for m in range(TPC):
        rm = r_sb[:, m * 2048:(m + 1) * 2048]
        s.op(P, "scalar", "act", lambda e, rm=rm, m=m: e.activation(
            sq[:], rm, AF.Square, accum_out=sumsq[:, m:m + 1]))
        s.op(P, "vector", "dve", lambda e, rm=rm, m=m: e.reduce_sum(
            sums[:, m:m + 1], rm, axis=AX))
    a_sq = s.c["act"]
    s.selfw(P, "vector", "dve")
    s.op(P, "vector", "dve", lambda e: e.tensor_scalar(
        mean[:], sums[:], 1.0 / 2048, None, ALU.mult))
    s.w(P, "vector", "act", a_sq)
    s.op(P, "vector", "dve", lambda e: e.tensor_scalar(
        var[:], sumsq[:], 1.0 / 2048, None, ALU.mult))
    s.selfw(P, "vector", "dve")
    s.op(P, "vector", "dve", lambda e: e.tensor_tensor(
        rstd[:], mean[:], mean[:], ALU.mult))
    s.selfw(P, "vector", "dve")
    d_st = s.op(P, "vector", "dve", lambda e: e.tensor_tensor(
        var[:], var[:], rstd[:], ALU.subtract))
    s.w(P, "scalar", "dve", d_st)
    a_sd = s.op(P, "scalar", "act", lambda e: e.activation(
        var[:], var[:], AF.Sqrt, bias=eps_ap))
    s.w(P, "vector", "act", a_sd)
    s.op(P, "vector", "dve", lambda e: e.reciprocal(rstd[:], var[:]))
    s.selfw(P, "vector", "dve")
    globals()["_LN_DBG_T"] = dict(
        sums=sums, sumsq=sumsq, mean=mean, var=var, rstd=rstd)

    tp_pe = [0, 0]
    ev_dve = [0, 0]
    xn_pe = [0, 0]
    for m in range(TPC):
        rm = r_sb[:, m * 2048:(m + 1) * 2048]
        xb = xn[m % 2]
        if m >= 2:
            s.w(P, "vector", "pe", xn_pe[m % 2])
        d_xn = s.op(P, "vector", "dve", lambda e, rm=rm, m=m, xb=xb:
                    e.tensor_scalar(xb[:], rm, mean[:, m:m + 1],
                                    rstd[:, m:m + 1], ALU.subtract, ALU.mult))
        s.w(P, "tensor", "dve", d_xn)
        for g in range(4):
            pt = tp[g % 2]
            s.w(P, "tensor", "dve", ev_dve[g % 2])
            for f4 in range(4):
                f = g * 4 + f4
                p_t = s.op(P, "tensor", "pe", lambda e, xb=xb, f=f, f4=f4, pt=pt:
                           e.transpose(pt[:, f4 * 128:(f4 + 1) * 128],
                                       xb[:, f * 128:(f + 1) * 128], ident[:]))
            tp_pe[g % 2] = p_t
            s.w(P, "vector", "pe", p_t)
            ev_dve[g % 2] = s.op(P, "vector", "dve", lambda e, g=g, m=m, pt=pt:
                                 e.tensor_copy(
                xnt.ap().rearrange("p (f t) -> p f t", f=16)
                    [:, g * 4:(g + 1) * 4, m * 128:(m + 1) * 128],
                pt.ap().rearrange("p (f t) -> p f t", f=4)))
        xn_pe[m % 2] = s.c["pe"]
    return s.c["dve"], s.c["pe"]


def em_proj(nc, P, s, st, name, w_dram, n_out, xnt, xnt_rdy_dve, epilogue,
            oc_grp=2, npsum=2, tok0=0, ntok=1024):
    noc = n_out // 128
    ngrp = (noc + oc_grp - 1) // oc_grp
    nn = ntok // 512
    ps = [st.enter_context(nc.psum_tensor(name + f"ps{i}", [128, ntok], F32))
          for i in range(npsum)]
    em_proj.last_ps = ps
    wb = [st.enter_context(
        nc.sbuf_tensor(name + f"wb{i}", [128, 16 * 128 * oc_grp], BF16))
        for i in range(2)]
    wtgt = [None] * ngrp
    grp_pe = [0] * ngrp
    ps_free = [0] * npsum
    wr = w_dram.rearrange("(kc p) m -> p kc m", p=128)

    def emit_dma(g):
        o0 = g * oc_grp
        ow = min(oc_grp, noc - o0)
        if g >= 2:
            s.w(P, "gpsimd", "pe", grp_pe[g - 2])
        wtgt[g] = s.op(P, "gpsimd", "wdma", lambda e, g=g, o0=o0, ow=ow:
                       e.dma_start(
            wb[g % 2].ap().rearrange("p (kc m) -> p kc m", kc=16)[:, :, :ow * 128],
            wr[:, :, o0 * 128:(o0 + ow) * 128]))

    emit_dma(0)
    if ngrp > 1:
        emit_dma(1)
    s.w(P, "tensor", "dve", xnt_rdy_dve)
    for oc in range(noc):
        g = oc // oc_grp
        oo = oc - g * oc_grp
        sl = oc % npsum
        if oo == 0:
            s.w(P, "tensor", "wdma", wtgt[g])
        s.w(P, "tensor", "dve", ps_free[sl])
        pe_c = 0
        for kc in range(16):
            for n in range(nn):
                pe_c = s.op(P, "tensor", "pe", lambda e, g=g, oo=oo, kc=kc,
                            n=n, sl=sl: e.matmul(
                    ps[sl][:, n * 512:(n + 1) * 512],
                    wb[g % 2][:, (kc * oc_grp + oo) * 128:(kc * oc_grp + oo + 1) * 128],
                    xnt[:, kc * 1024 + tok0 + n * 512: kc * 1024 + tok0 + (n + 1) * 512],
                    start=(kc == 0), stop=(kc == 15)))
        grp_pe[g] = pe_c
        fr = epilogue(oc, ps[sl], pe_c)
        if isinstance(fr, tuple):
            for ii in range(npsum):
                ps_free[ii] = fr[ii % len(fr)]
        elif fr is not None:
            ps_free[sl] = fr
        if oo == oc_grp - 1 and g + 2 < ngrp:
            emit_dma(g + 2)
    return s.c["dve"]


def ep_copy_bf16(nc, P, s, dst_sb, ntok=1024):
    def ep(oc, ps, pe_cnt):
        s.w(P, "vector", "pe", pe_cnt)
        return s.op(P, "vector", "dve", lambda e, oc=oc, ps=ps: e.tensor_copy(
            dst_sb[:, oc * ntok:(oc + 1) * ntok], ps[:]))
    return ep


def em_rope_pair(nc, P, s, scr, dst_sb, oc_pair, psA, psB, cosT, sinT,
                 pe_cnt, ntok=1024):
    ocA, ocB = oc_pair
    t1 = scr[:, 0:ntok]
    t2 = scr[:, ntok:2 * ntok]
    s.w(P, "vector", "pe", pe_cnt)
    s.selfw(P, "vector", "dve")
    s.op(P, "vector", "dve", lambda e: e.tensor_tensor(t1, psA[:], cosT, ALU.mult))
    s.op(P, "vector", "dve", lambda e: e.tensor_tensor(t2, psB[:], sinT, ALU.mult))
    s.selfw(P, "vector", "dve")
    s.op(P, "vector", "dve", lambda e: e.tensor_tensor(
        dst_sb[:, ocA * ntok:(ocA + 1) * ntok], t1, t2, ALU.subtract))
    s.op(P, "vector", "dve", lambda e: e.tensor_tensor(t1, psB[:], cosT, ALU.mult))
    s.op(P, "vector", "dve", lambda e: e.tensor_tensor(t2, psA[:], sinT, ALU.mult))
    s.selfw(P, "vector", "dve")
    return s.op(P, "vector", "dve", lambda e: e.tensor_tensor(
        dst_sb[:, ocB * ntok:(ocB + 1) * ntok], t1, t2, ALU.add))


def em_fm_to_tm(nc, P, s, st, name, src_fm, dst_tm, noc, src_rdy_dve,
                ident, ntok=1024, add_into=False):
    n_out = noc * 128
    tp = [st.enter_context(nc.psum_tensor(name + f"tp{i}", [128, 512], BF16))
          for i in range(2)]
    ev_dve = [0, 0]
    s.w(P, "tensor", "dve", src_rdy_dve)
    for m in range(ntok // 128):
        for g in range((noc + 3) // 4):
            gw = min(4, noc - g * 4)
            pt = tp[(m * ((noc + 3) // 4) + g) % 2]
            s.w(P, "tensor", "dve", ev_dve[(m * ((noc + 3) // 4) + g) % 2])
            for f4 in range(gw):
                f = g * 4 + f4
                p_t = s.op(P, "tensor", "pe", lambda e, m=m, f=f, f4=f4, pt=pt:
                           e.transpose(pt[:, f4 * 128:(f4 + 1) * 128],
                                       src_fm[:, f * ntok + m * 128:
                                              f * ntok + (m + 1) * 128],
                                       ident[:]))
            s.w(P, "vector", "pe", p_t)
            if add_into:
                s.selfw(P, "vector", "dve")
                ev_dve[(m * ((noc + 3) // 4) + g) % 2] = s.op(
                    P, "vector", "dve", lambda e, m=m, g=g, gw=gw, pt=pt:
                    e.tensor_tensor(
                        dst_tm[:, m * n_out + g * 512: m * n_out + g * 512 + gw * 128],
                        dst_tm[:, m * n_out + g * 512: m * n_out + g * 512 + gw * 128],
                        pt[:, :gw * 128], ALU.add))
            else:
                ev_dve[(m * ((noc + 3) // 4) + g) % 2] = s.op(
                    P, "vector", "dve", lambda e, m=m, g=g, gw=gw, pt=pt:
                    e.tensor_copy(
                        dst_tm[:, m * n_out + g * 512: m * n_out + g * 512 + gw * 128],
                        pt[:, :gw * 128]))
    return s.c["dve"]



def build_l1attn_kv(debug=False):
    nc = bass.Bass("TRN2", target_bir_lowering=False)
    r_d = nc.dram_tensor("r", [1024, 2048], F32, kind="ExternalInput")
    wk_d = nc.dram_tensor("wkT", [2048, 2048], F32, kind="ExternalInput")
    wv_d = nc.dram_tensor("wvT", [2048, 2048], F32, kind="ExternalInput")
    id_d = nc.dram_tensor("ident", [128, 128], BF16, kind="ExternalInput")
    k_o = nc.dram_tensor("k_tm", [1024, 2048], BF16, kind="ExternalOutput")
    v_o = nc.dram_tensor("v_tm", [1024, 2048], BF16, kind="ExternalOutput")
    if debug:
        xnt_o = nc.dram_tensor("xnt_o", [128, 16 * 1024], BF16, kind="ExternalOutput")
        fm_o = nc.dram_tensor("fm_o", [128, 16 * 1024], BF16, kind="ExternalOutput")

    with ExitStack() as st:
        s = S(nc, st)
        r_sb = st.enter_context(nc.sbuf_tensor("r_sb", [128, 8 * 2048], F32))
        ident = st.enter_context(nc.sbuf_tensor("ident_sb", [128, 128], BF16))
        xnt = st.enter_context(nc.sbuf_tensor("xnt", [128, 16 * 1024], BF16))
        fm = st.enter_context(nc.sbuf_tensor("fm", [128, 16 * 1024], BF16))
        tm = st.enter_context(nc.sbuf_tensor("tm", [128, 8 * 2048], BF16))

        P = Phase()
        r_rdy = em_load_r(nc, P, s, r_sb, r_d)
        id_rdy = s.op(P, "sync", "dma", lambda e: e.dma_start(ident[:], id_d[:]))
        cst, _ = em_consts(nc, P, s, st, [1e-5])
        flush(nc, P)

        with ExitStack() as stln:
            P = Phase()
            s.w(P, "tensor", "dma", id_rdy)
            xnt_dve, _ = em_ln(nc, P, s, stln, "ln", r_sb, xnt, ident, r_rdy,
                               cst[:, 0:1])
            flush(nc, P)

        for nm, (w_d, o_d) in (("k", (wk_d, k_o)), ("v", (wv_d, v_o))):
            with ExitStack() as st2:
                P = Phase()
                em_proj(nc, P, s, st2, "pj" + nm, w_d, 2048, xnt, xnt_dve,
                        ep_copy_bf16(nc, P, s, fm))
                flush(nc, P)
                P = Phase()
                d = em_fm_to_tm(nc, P, s, st2, "tt" + nm, fm, tm, 16, s.c["dve"],
                                ident)
                s.w(P, "sync", "dve", d)
                v = s.op(P, "sync", "dma", lambda e, o_d=o_d: e.dma_start(
                    o_d.rearrange("(m p) d -> p m d", p=128),
                    tm.ap().rearrange("p (m d) -> p m d", m=TPC)))
                s.w(P, "sync", "dma", v)
                if debug and nm == "k":
                    v2 = s.op(P, "sync", "dma", lambda e: e.dma_start(
                        xnt_o[:], xnt[:]))
                    v2 = s.op(P, "sync", "dma", lambda e: e.dma_start(
                        fm_o[:], fm[:]))
                    s.w(P, "sync", "dma", v2)
                flush(nc, P)
    nc.finalize()
    return nc



def _f32c(x):
    return np.ascontiguousarray(np.asarray(x), dtype=np.float32)


def core_rows(core):
    b, c = divmod(core, 4)
    return b, TILES[c]


def shard_rows(x_full):
    out = []
    for core in range(NCORES):
        b, tiles = core_rows(core)
        out.append(np.ascontiguousarray(np.concatenate(
            [x_full[b, t * 128:(t + 1) * 128] for t in tiles], axis=0)))
    return out


def unshard_rows(parts, feat):
    out = np.empty((B, T, feat), parts[0].dtype)
    for core in range(NCORES):
        b, tiles = core_rows(core)
        for j, t in enumerate(tiles):
            out[b, t * 128:(t + 1) * 128] = parts[core][j * 128:(j + 1) * 128]
    return out


def host_consts():
    return np.eye(128, dtype=np.float32).astype(BFD)


def host_masks(c):
    mk = np.empty((8, 128, 512), np.float32)
    i = np.arange(128)[:, None]
    kk = np.arange(512)[None, :]
    for j, t in enumerate(TILES[c]):
        off = (t % 4) * 128
        mk[j] = np.where(kk <= off + i, 0.0, NEG)
    return mk


_prog_cache = {}


def _get(key, builder):
    if key not in _prog_cache:
        _prog_cache[key] = builder()
    return _prog_cache[key]


def _prep_weights(inputs):
    w = {}
    w["l0_guT"] = np.ascontiguousarray(np.concatenate(
        [np.asarray(inputs["L0_ffn_gate_up_proj"]).transpose(0, 2, 1),
         np.concatenate([np.asarray(inputs["L0_ffn_shared_gate_w"]),
                         np.asarray(inputs["L0_ffn_shared_up_w"])], 0).T[None]],
        0), dtype=np.float32)
    w["l0_dnT"] = np.ascontiguousarray(np.concatenate(
        [np.asarray(inputs["L0_ffn_down_proj"]).transpose(0, 2, 1),
         np.asarray(inputs["L0_ffn_shared_down_w"]).T[None]], 0),
        dtype=np.float32)
    w["l0_rwT"] = np.ascontiguousarray(np.asarray(inputs["L0_ffn_router_w"]).T,
                                       dtype=np.float32)
    w["l0_ebb"] = np.ascontiguousarray(np.broadcast_to(
        np.asarray(inputs["L0_ffn_e_bias"], dtype=np.float32), (128, 16)))
    w["l0_wq"] = _f32c(inputs["L0_attn_Wq"])
    w["l0_wk"] = _f32c(inputs["L0_attn_Wk"])
    w["l0_wv"] = _f32c(inputs["L0_attn_Wv"])
    w["l0_wo"] = _f32c(inputs["L0_attn_Wo"])
    w["l0_hsb"] = np.ascontiguousarray(np.broadcast_to(
        np.asarray(inputs["L0_attn_head_scales"], dtype=np.float32), (128, 8)))
    w["l1_wqT"] = np.ascontiguousarray(np.asarray(inputs["L1_attn_Wq"]).T,
                                       dtype=np.float32)
    w["l1_wkT"] = np.ascontiguousarray(np.asarray(inputs["L1_attn_Wk"]).T,
                                       dtype=np.float32)
    w["l1_wvT"] = np.ascontiguousarray(np.asarray(inputs["L1_attn_Wv"]).T,
                                       dtype=np.float32)
    w["l1_woT"] = np.ascontiguousarray(np.asarray(inputs["L1_attn_Wo"]).T,
                                       dtype=np.float32)
    wqkv = np.asarray(inputs["L1_ffn_W_qkv"], dtype=np.float32)
    w["lf_wqT"] = np.ascontiguousarray(wqkv[:2048].T)
    w["lf_wkvT"] = np.ascontiguousarray(wqkv[2048:].T)
    w["lf_wgT"] = np.ascontiguousarray(np.asarray(inputs["L1_ffn_W_gate"]).T,
                                       dtype=np.float32)
    sel16 = np.zeros((16, 2048), np.float32)
    for e in range(16):
        sel16[e, e * 128:(e + 1) * 128] = 1.0
    w["sel16"] = sel16.astype(BFD)
    w["ident"] = host_consts()
    return w


def _assemble(parts, key, feat):
    out = np.zeros((B, T, feat), np.asarray(parts[0][key]).dtype)
    for core in range(NCORES):
        b, tiles = core_rows(core)
        got = np.asarray(parts[core][key])
        for j, t in enumerate(tiles):
            out[b, t * 128:(t + 1) * 128] = got[j * 128:(j + 1) * 128]
    return out


def kernel(**inputs):
    w = _prep_weights(inputs)
    x = np.asarray(inputs["L0_attn_x"], dtype=np.float32)
    r = shard_rows(x)
    masks = [host_masks(core % 4) for core in range(NCORES)]
    masksb = [m.astype(BFD) for m in masks]
    ropes = [host_rope_tables(core % 4) for core in range(NCORES)]
    drp = host_dirs_rep()

    def run(key, builder, maps):
        nc = _get(key, builder)
        return run_bass_kernel_spmd(nc, maps, core_ids=list(range(NCORES))).results

    def moe_pass():
        res = run("moe", build_moe, [
            {"r": r[c], "gate_upT": w["l0_guT"], "downT": w["l0_dnT"],
             "router_wT": w["l0_rwT"], "ebb": w["l0_ebb"],
             "sel16": w["sel16"], "ident": w["ident"]}
            for c in range(NCORES)])
        for c in range(NCORES):
            r[c] = np.asarray(res[c]["r_out"])

    def l0attn_pass():
        res = run("l0kv", build_l0attn_kv, [
            {"r": r[c], "wk": w["l0_wk"], "wv": w["l0_wv"],
             "cosT": ropes[c][0], "sinT": ropes[c][1], "ident": w["ident"]}
            for c in range(NCORES)])
        k_full = _assemble(res, "k_ro", 512)
        v_full = _assemble(res, "v_tm", 512)
        res = run("l0main", build_l0attn_main, [
            {"r": r[c], "wq": w["l0_wq"], "wo": w["l0_wo"],
             "k_ro": np.ascontiguousarray(k_full[core_rows(c)[0]]),
             "v_tm": np.ascontiguousarray(v_full[core_rows(c)[0]]),
             "cosT": ropes[c][0], "sinT": ropes[c][1], "masksb": masksb[c],
             "hsb": w["l0_hsb"], "dirs_rep": drp, "ident": w["ident"]}
            for c in range(NCORES)])
        for c in range(NCORES):
            r[c] = np.asarray(res[c]["r_out"])

    def l1ffn_pass():
        res = run("lfkv", build_l1ffn_kv, [
            {"r": r[c], "wkvT": w["lf_wkvT"], "wgT": w["lf_wgT"],
             "ident": w["ident"]} for c in range(NCORES)])
        ks_full = _assemble(res, "ks", 128)
        vs_full = _assemble(res, "vs", 128)
        res = run("lfmain", build_l1ffn_main, [
            {"r": r[c], "wqT": w["lf_wqT"],
             "ks": np.ascontiguousarray(ks_full[core_rows(c)[0]]),
             "vs": np.ascontiguousarray(vs_full[core_rows(c)[0]]),
             "masks": masks[c], "ident": w["ident"]} for c in range(NCORES)])
        for c in range(NCORES):
            r[c] = np.asarray(res[c]["r_out"])

    def l1attn_pass():
        res = run("lakv", build_l1attn_kv, [
            {"r": r[c], "wkT": w["l1_wkT"], "wvT": w["l1_wvT"],
             "ident": w["ident"]} for c in range(NCORES)])
        k_full = _assemble(res, "k_tm", 2048)
        v_full = _assemble(res, "v_tm", 2048)
        res = run("lamain", build_l1attn_main, [
            {"r": r[c], "wqT": w["l1_wqT"], "woT": w["l1_woT"],
             "k_tm": np.ascontiguousarray(k_full[core_rows(c)[0]]),
             "v_tm": np.ascontiguousarray(v_full[core_rows(c)[0]]),
             "masks": masks[c], "ident": w["ident"]} for c in range(NCORES)])
        for c in range(NCORES):
            r[c] = np.asarray(res[c]["r_out"])

    moe_pass()
    l0attn_pass()
    moe_pass()
    l1ffn_pass()
    l1attn_pass()
    l1ffn_pass()

    return unshard_rows(r, 2048)


def build_dbg_ln():
    nc = bass.Bass("TRN2", target_bir_lowering=False)
    r_d = nc.dram_tensor("r", [1024, 2048], F32, kind="ExternalInput")
    id_d = nc.dram_tensor("ident", [128, 128], BF16, kind="ExternalInput")
    xnt_o = nc.dram_tensor("xnt_o", [128, 16 * 1024], BF16, kind="ExternalOutput")
    st_o = nc.dram_tensor("st_o", [128, 40], F32, kind="ExternalOutput")
    with ExitStack() as st:
        s = S(nc, st)
        r_sb = st.enter_context(nc.sbuf_tensor("r_sb", [128, 8 * 2048], F32))
        ident = st.enter_context(nc.sbuf_tensor("ident_sb", [128, 128], BF16))
        xnt = st.enter_context(nc.sbuf_tensor("xnt", [128, 16 * 1024], BF16))
        P = Phase()
        r_rdy = em_load_r(nc, P, s, r_sb, r_d)
        id_rdy = s.op(P, "sync", "dma", lambda e: e.dma_start(ident[:], id_d[:]))
        cst, _ = em_consts(nc, P, s, st, [1e-5])
        flush(nc, P)
        with ExitStack() as stln:
            P = Phase()
            s.w(P, "tensor", "dma", id_rdy)
            em_ln(nc, P, s, stln, "ln", r_sb, xnt, ident, r_rdy, cst[:, 0:1])
            t = globals()["_LN_DBG_T"]
            s.w(P, "sync", "dve", s.c["dve"])
            s.w(P, "sync", "act", s.c["act"])
            for i, nmm in enumerate(("sums", "sumsq", "mean", "var", "rstd")):
                s.op(P, "sync", "dma", lambda e, i=i, nmm=nmm: e.dma_start(
                    st_o[:, i * 8:(i + 1) * 8], t[nmm][:]))
            s.w(P, "sync", "dma", s.c["dma"])
            flush(nc, P)
        P = Phase()
        v2 = s.op(P, "sync", "dma", lambda e: e.dma_start(xnt_o[:], xnt[:]))
        s.w(P, "sync", "dma", v2)
        flush(nc, P)
    nc.finalize()
    return nc

_LN_DBG = {}



def em_attn_head_rows(nc, P, s, st, name, heads, qt_ap, kT_ap, v_ap, msk_sb,
                      scale, out_cb, kv_wait=None, order_hj=True):
    pbuf = st.enter_context(nc.sbuf_tensor(name + "pb", [128, 4096], BF16))
    ptb = st.enter_context(nc.sbuf_tensor(name + "pt", [128, 4096], BF16))
    sumsp = st.enter_context(nc.sbuf_tensor(name + "sp", [128, 8], F32))
    rec = st.enter_context(nc.sbuf_tensor(name + "rc", [128, 1], F32))
    s_ps = [st.enter_context(nc.psum_tensor(name + f"sps{i}", [128, 512], F32))
            for i in range(2)]
    t_ps = [st.enter_context(nc.psum_tensor(name + f"tps{i}", [128, 512], BF16))
            for i in range(2)]
    y_ps = [st.enter_context(nc.psum_tensor(name + f"yps{i}", [128, 128], F32))
            for i in range(2)]
    exp_done = [0, 0]
    tev_done = [0, 0]
    y_free = [0, 0]
    yi = 0
    prev_red = 0
    loop = [(h, j) for h in heads for j in range(TPC)] if order_hj else \
           [(h, j) for j in range(TPC) for h in heads]
    for h, j in loop:
        if kv_wait is not None and j == 0:
            kv_wait(h)
        nkb = j + 1
        for kb in range(nkb):
            sp = s_ps[kb % 2]
            s.w(P, "tensor", "act", exp_done[kb % 2])
            if kb == 0:
                s.w(P, "scalar", "dve", prev_red)
            pe_c = s.op(P, "tensor", "pe", lambda e, h=h, j=j, kb=kb, sp=sp:
                        e.matmul(sp[:], qt_ap(h, j), kT_ap(h)[:, kb * 512:(kb + 1) * 512],
                                 start=True, stop=True))
            if kb == j:
                s.w(P, "vector", "pe", pe_c)
                d_m = s.op(P, "vector", "dve", lambda e, j=j, sp=sp:
                           e.tensor_tensor(sp[:], sp[:],
                                           msk_sb[:, j * 512:(j + 1) * 512], ALU.add))
                s.w(P, "scalar", "dve", d_m)
            else:
                s.w(P, "scalar", "pe", pe_c)
            exp_done[kb % 2] = s.op(P, "scalar", "act", lambda e, kb=kb, sp=sp:
                                    e.activation(pbuf[:, kb * 512:(kb + 1) * 512],
                                                 sp[:], AF.Exp, scale=scale,
                                                 accum_out=sumsp[:, kb:kb + 1]))
        a_exp = s.c["act"]
        s.w(P, "vector", "act", a_exp)
        prev_red = s.op(P, "vector", "dve", lambda e, nkb=nkb: e.reduce_sum(
            rec[:], sumsp[:, 0:nkb], axis=AX))
        s.selfw(P, "vector", "dve")
        d_rec = s.op(P, "vector", "dve", lambda e: e.reciprocal(rec[:], rec[:]))
        s.w(P, "tensor", "act", a_exp)
        for g in range(nkb):
            pt = t_ps[g % 2]
            s.w(P, "tensor", "dve", tev_done[g % 2])
            for kc in range(4):
                t_c = s.op(P, "tensor", "pe", lambda e, g=g, kc=kc, pt=pt:
                           e.transpose(pt[:, kc * 128:(kc + 1) * 128],
                                       pbuf[:, (g * 4 + kc) * 128:(g * 4 + kc + 1) * 128],
                                       _IDENT[0][:]))
            s.w(P, "vector", "pe", t_c)
            tev_done[g % 2] = s.op(P, "vector", "dve", lambda e, g=g, pt=pt:
                                   e.tensor_copy(ptb[:, g * 512:(g + 1) * 512], pt[:]))
        d_pt = s.c["dve"]
        yp = y_ps[yi]
        s.w(P, "tensor", "dve", max(d_pt, y_free[yi]))
        for t in range(nkb * 4):
            pe_c = s.op(P, "tensor", "pe", lambda e, h=h, t=t, yp=yp:
                        e.matmul(yp[:], ptb[:, t * 128:(t + 1) * 128], v_ap(h, t),
                                 start=(t == 0), stop=(t == nkb * 4 - 1)))
        y_free[yi] = out_cb(h, j, yp, rec[:], pe_c, d_rec)
        yi ^= 1
    return s.c["dve"]


def build_l1ffn_main():
    nc = bass.Bass("TRN2", target_bir_lowering=False)
    r_d = nc.dram_tensor("r", [1024, 2048], F32, kind="ExternalInput")
    wq_d = nc.dram_tensor("wqT", [2048, 2048], F32, kind="ExternalInput")
    ks_d = nc.dram_tensor("ks", [4096, 128], BF16, kind="ExternalInput")
    vs_d = nc.dram_tensor("vs", [4096, 128], BF16, kind="ExternalInput")
    mk_d = nc.dram_tensor("masks", [8, 128, 512], F32, kind="ExternalInput")
    id_d = nc.dram_tensor("ident", [128, 128], BF16, kind="ExternalInput")
    r_o = nc.dram_tensor("r_out", [1024, 2048], F32, kind="ExternalOutput")

    with ExitStack() as st:
        s = S(nc, st)
        r_sb = st.enter_context(nc.sbuf_tensor("r_sb", [128, 8 * 2048], F32))
        ident = st.enter_context(nc.sbuf_tensor("id_sb", [128, 128], BF16))
        _IDENT[0] = ident
        qt = st.enter_context(nc.sbuf_tensor("qt", [128, 16 * 1024], BF16))
        ksT = st.enter_context(nc.sbuf_tensor("ksT", [128, 4096], BF16))
        vst = st.enter_context(nc.sbuf_tensor("vst", [128, 32 * 128], BF16))
        msk = st.enter_context(nc.sbuf_tensor("msk", [128, 8 * 512], F32))
        hr = st.enter_context(nc.sbuf_tensor("hr", [128, 1], F32))

        P = Phase()
        r_rdy = em_load_r(nc, P, s, r_sb, r_d)
        id_rdy = s.op(P, "sync", "dma", lambda e: e.dma_start(ident[:], id_d[:]))
        s.op(P, "sync", "dma", lambda e: e.dma_start(
            msk.ap().rearrange("p (j k) -> p j k", j=8),
            mk_d.rearrange("j p k -> p j k")))
        s.op(P, "scalar", "dma", lambda e: e.dma_start_transpose(ksT[:], ks_d[:]))
        kv_rdy = s.op(P, "sync", "dma", lambda e: e.dma_start(
            vst.ap().rearrange("p (t d) -> p t d", t=32),
            vs_d.rearrange("(t p) d -> p t d", p=128)))
        cst, _ = em_consts(nc, P, s, st, [1e-5])
        flush(nc, P)

        with ExitStack() as st2:
            xnt = st2.enter_context(nc.sbuf_tensor("xnt", [128, 16 * 1024], BF16))
            with ExitStack() as stln:
                P = Phase()
                s.w(P, "tensor", "dma", id_rdy)
                xnt_dve, _ = em_ln(nc, P, s, stln, "ln", r_sb, xnt, ident,
                                   r_rdy, cst[:, 0:1])
                flush(nc, P)
            P = Phase()
            em_proj(nc, P, s, st2, "pq", wq_d, 2048, xnt, xnt_dve,
                    ep_copy_bf16(nc, P, s, qt))
            flush(nc, P)

        with ExitStack() as st3:
            P = Phase()
            s.w(P, "tensor", "dma", kv_rdy)
            s.w(P, "scalar", "dma", kv_rdy)
            s.w(P, "vector", "dma", kv_rdy)

            def out_cb(h, j, yp, rec_ap, pe_c, d_rec):
                s.w(P, "vector", "pe", pe_c)
                s.w(P, "vector", "dve", d_rec)
                s.selfw(P, "vector", "dve")
                s.op(P, "vector", "dve", lambda e: e.tensor_scalar(
                    hr[:], rec_ap, 0.5, None, ALU.mult))
                s.selfw(P, "vector", "dve")
                return s.op(P, "vector", "dve", lambda e, h=h, j=j, yp=yp:
                            e.scalar_tensor_tensor(
                    r_sb[:, j * 2048 + h * 128: j * 2048 + (h + 1) * 128],
                    yp[:], hr[:],
                    r_sb[:, j * 2048 + h * 128: j * 2048 + (h + 1) * 128],
                    ALU.mult, ALU.add))

            em_attn_head_rows(
                nc, P, s, st3, "at", list(range(16)),
                qt_ap=lambda h, j: qt[:, h * 1024 + j * 128: h * 1024 + (j + 1) * 128],
                kT_ap=lambda h: ksT,
                v_ap=lambda h, t: vst[:, t * 128:(t + 1) * 128],
                msk_sb=msk, scale=1.0 / math.sqrt(128.0), out_cb=out_cb)
            flush(nc, P)

        P = Phase()
        em_store_r(nc, P, s, r_o, r_sb,
                   [("dve", s.c["dve"]), ("act", s.c["act"]), ("pe", s.c["pe"])])
        flush(nc, P)
    nc.finalize()
    return nc


_IDENT = [None]


def build_l1ffn_kv():
    nc = bass.Bass("TRN2", target_bir_lowering=False)
    r_d = nc.dram_tensor("r", [1024, 2048], F32, kind="ExternalInput")
    wkv_d = nc.dram_tensor("wkvT", [2048, 4096], F32, kind="ExternalInput")
    wg_d = nc.dram_tensor("wgT", [2048, 16], F32, kind="ExternalInput")
    id_d = nc.dram_tensor("ident", [128, 128], BF16, kind="ExternalInput")
    ks_o = nc.dram_tensor("ks", [1024, 128], BF16, kind="ExternalOutput")
    vs_o = nc.dram_tensor("vs", [1024, 128], BF16, kind="ExternalOutput")

    with ExitStack() as st:
        s = S(nc, st)
        r_sb = st.enter_context(nc.sbuf_tensor("r_sb", [128, 8 * 2048], F32))
        ident = st.enter_context(nc.sbuf_tensor("id_sb", [128, 128], BF16))
        _IDENT[0] = ident
        wgt = st.enter_context(nc.sbuf_tensor("wgt", [128, 16 * 16], BF16))
        ind = st.enter_context(nc.sbuf_tensor("ind", [128, 8 * 16], F32))
        gtw = st.enter_context(nc.sbuf_tensor("gtw", [128, 8 * 16], F32))
        gex = st.enter_context(nc.sbuf_tensor("gex", [128, 16], F32))
        gst = st.enter_context(nc.sbuf_tensor("gst", [128, 4], F32))
        kst = st.enter_context(nc.sbuf_tensor("kst", [128, 8 * 128], BF16))
        vst_ = st.enter_context(nc.sbuf_tensor("vst", [128, 8 * 128], BF16))
        g_ps = st.enter_context(nc.psum_tensor("g_ps", [128, 16], F32))

        P = Phase()
        r_rdy = em_load_r(nc, P, s, r_sb, r_d)
        id_rdy = s.op(P, "sync", "dma", lambda e: e.dma_start(ident[:], id_d[:]))
        wg_rdy = s.op(P, "gpsimd", "wdma", lambda e: e.dma_start(
            wgt.ap().rearrange("p (kc m) -> p kc m", kc=16),
            wg_d.rearrange("(kc p) m -> p kc m", p=128)))
        cst, _ = em_consts(nc, P, s, st, [1e-5])
        flush(nc, P)

        with ExitStack() as st2:
            xnt = st2.enter_context(nc.sbuf_tensor("xnt", [128, 16 * 1024], BF16))
            with ExitStack() as stln:
                P = Phase()
                s.w(P, "tensor", "dma", id_rdy)
                xnt_dve, _ = em_ln(nc, P, s, stln, "ln", r_sb, xnt, ident,
                                   r_rdy, cst[:, 0:1])
                flush(nc, P)

            P = Phase()
            s.w(P, "tensor", "dve", xnt_dve)
            s.w(P, "tensor", "wdma", wg_rdy)
            prev = 0
            for m in range(TPC):
                s.w(P, "tensor", "dve", prev)
                for kc in range(16):
                    pe_c = s.op(P, "tensor", "pe", lambda e, m=m, kc=kc:
                                e.matmul(g_ps[:],
                                         xnt[:, kc * 1024 + m * 128: kc * 1024 + (m + 1) * 128],
                                         wgt[:, kc * 16:(kc + 1) * 16],
                                         start=(kc == 0), stop=(kc == 15)))
                s.w(P, "scalar", "pe", pe_c)
                s.w(P, "scalar", "dve", prev)
                a_e = s.op(P, "scalar", "act", lambda e, m=m: e.activation(
                    gex[:], g_ps[:], AF.Exp, accum_out=gst[:, 0:1]))
                s.w(P, "vector", "act", a_e)
                s.op(P, "vector", "dve", lambda e: e.reduce_max(
                    gst[:, 1:2], gex[:], axis=AX))
                s.selfw(P, "vector", "dve")
                s.op(P, "vector", "dve", lambda e: e.reciprocal(
                    gst[:, 2:3], gst[:, 0:1]))
                s.selfw(P, "vector", "dve")
                s.op(P, "vector", "dve", lambda e, m=m: e.tensor_scalar(
                    ind[:, m * 16:(m + 1) * 16], gex[:], gst[:, 1:2], None,
                    ALU.is_ge))
                s.op(P, "vector", "dve", lambda e: e.tensor_tensor(
                    gst[:, 3:4], gst[:, 1:2], gst[:, 2:3], ALU.mult))
                s.selfw(P, "vector", "dve")
                prev = s.op(P, "vector", "dve", lambda e, m=m: e.tensor_scalar(
                    gtw[:, m * 16:(m + 1) * 16], ind[:, m * 16:(m + 1) * 16],
                    gst[:, 3:4], None, ALU.mult))
            flush(nc, P)

            for nm, (col0, sel, dst) in (("k", (0, ind, kst)), ("v", (2048, gtw, vst_))):
                with ExitStack() as st3:
                    fm = st3.enter_context(nc.sbuf_tensor(nm + "fm", [128, 16 * 1024], BF16))
                    tm = st3.enter_context(nc.sbuf_tensor(nm + "tm", [128, 8 * 2048], BF16))
                    P = Phase()
                    wslice = wkv_d[:, col0:col0 + 2048]
                    em_proj(nc, P, s, st3, "pj" + nm, wslice, 2048, xnt, xnt_dve,
                            ep_copy_bf16(nc, P, s, fm))
                    flush(nc, P)
                    P = Phase()
                    d_t = em_fm_to_tm(nc, P, s, st3, "tt" + nm, fm, tm, 16,
                                      s.c["dve"], ident)
                    s.w(P, "vector", "dve", d_t)
                    for m in range(TPC):
                        for h in range(16):
                            sc = sel[:, m * 16 + h: m * 16 + h + 1]
                            src = tm[:, m * 2048 + h * 128: m * 2048 + (h + 1) * 128]
                            dstm = dst[:, m * 128:(m + 1) * 128]
                            if h == 0:
                                s.op(P, "vector", "dve", lambda e, src=src, sc=sc,
                                     dstm=dstm: e.tensor_scalar(
                                         dstm, src, sc, None, ALU.mult))
                            else:
                                s.selfw(P, "vector", "dve")
                                s.op(P, "vector", "dve", lambda e, src=src, sc=sc,
                                     dstm=dstm: e.scalar_tensor_tensor(
                                         dstm, src, sc, dstm, ALU.mult, ALU.add))
                    d_sel = s.c["dve"]
                    o_d = ks_o if nm == "k" else vs_o
                    s.w(P, "sync", "dve", d_sel)
                    v = s.op(P, "sync", "dma", lambda e, o_d=o_d, dst=dst:
                             e.dma_start(
                        o_d.rearrange("(m p) d -> p m d", p=128),
                        dst.ap().rearrange("p (m d) -> p m d", m=TPC)))
                    s.w(P, "sync", "dma", v)
                    flush(nc, P)
    nc.finalize()
    return nc


def build_l1attn_main(debug=False):
    nc = bass.Bass("TRN2", target_bir_lowering=False)
    r_d = nc.dram_tensor("r", [1024, 2048], F32, kind="ExternalInput")
    wq_d = nc.dram_tensor("wqT", [2048, 2048], F32, kind="ExternalInput")
    wo_d = nc.dram_tensor("woT", [2048, 2048], F32, kind="ExternalInput")
    k_d = nc.dram_tensor("k_tm", [4096, 2048], BF16, kind="ExternalInput")
    v_d = nc.dram_tensor("v_tm", [4096, 2048], BF16, kind="ExternalInput")
    mk_d = nc.dram_tensor("masks", [8, 128, 512], F32, kind="ExternalInput")
    id_d = nc.dram_tensor("ident", [128, 128], BF16, kind="ExternalInput")
    r_o = nc.dram_tensor("r_out", [1024, 2048], F32, kind="ExternalOutput")
    if debug:
        qt_o = nc.dram_tensor("qt_o", [128, 16 * 1024], BF16, kind="ExternalOutput")
        ao_o = nc.dram_tensor("ao_o", [128, 16 * 1024], BF16, kind="ExternalOutput")

    with ExitStack() as st:
        s = S(nc, st)
        r_sb = st.enter_context(nc.sbuf_tensor("r_sb", [128, 8 * 2048], F32))
        ident = st.enter_context(nc.sbuf_tensor("id_sb", [128, 128], BF16))
        _IDENT[0] = ident
        qt = st.enter_context(nc.sbuf_tensor("qt", [128, 16 * 1024], BF16))
        aoT = st.enter_context(nc.sbuf_tensor("aoT", [128, 16 * 1024], BF16))
        msk = st.enter_context(nc.sbuf_tensor("msk", [128, 8 * 512], F32))

        P = Phase()
        r_rdy = em_load_r(nc, P, s, r_sb, r_d)
        id_rdy = s.op(P, "sync", "dma", lambda e: e.dma_start(ident[:], id_d[:]))
        mk_rdy = s.op(P, "sync", "dma", lambda e: e.dma_start(
            msk.ap().rearrange("p (j k) -> p j k", j=8),
            mk_d.rearrange("j p k -> p j k")))
        cst, _ = em_consts(nc, P, s, st, [1e-5])
        flush(nc, P)

        with ExitStack() as st2:
            xnt = st2.enter_context(nc.sbuf_tensor("xnt", [128, 16 * 1024], BF16))
            with ExitStack() as stln:
                P = Phase()
                s.w(P, "tensor", "dma", id_rdy)
                xnt_dve, _ = em_ln(nc, P, s, stln, "ln", r_sb, xnt, ident,
                                   r_rdy, cst[:, 0:1])
                flush(nc, P)
            P = Phase()
            em_proj(nc, P, s, st2, "pq", wq_d, 2048, xnt, xnt_dve,
                    ep_copy_bf16(nc, P, s, qt))
            flush(nc, P)

        with ExitStack() as st3:
            kTb = [st3.enter_context(nc.sbuf_tensor(f"kTb{i}", [128, 4096], BF16))
                   for i in range(2)]
            vb = [st3.enter_context(nc.sbuf_tensor(f"vb{i}", [128, 32 * 128], BF16))
                  for i in range(2)]
            ysb = [st3.enter_context(nc.sbuf_tensor(f"ysb{i}", [128, 128], BF16))
                   for i in range(2)]
            yt_ps = [st3.enter_context(nc.psum_tensor(f"ytps{i}", [128, 128], BF16))
                     for i in range(2)]
            P = Phase()
            s.w(P, "vector", "dma", mk_rdy)

            kv_tgt = [0] * 16
            head_pe_done = [0] * 16

            def emit_kv_dma(h):
                hb = h % 2
                if h >= 2:
                    s.w(P, "scalar", "pe", head_pe_done[h - 2])
                    s.w(P, "sync", "pe", head_pe_done[h - 2])
                t1 = s.op(P, "scalar", "dma", lambda e, h=h, hb=hb:
                          e.dma_start_transpose(kTb[hb][:],
                                                k_d[:, h * 128:(h + 1) * 128]))
                t2 = s.op(P, "sync", "dma", lambda e, h=h, hb=hb: e.dma_start(
                    vb[hb].ap().rearrange("p (t d) -> p t d", t=32),
                    v_d.rearrange("(t p) d -> p t d", p=128)
                       [:, :, h * 128:(h + 1) * 128]))
                kv_tgt[h] = max(t1, t2)

            emit_kv_dma(0)
            emit_kv_dma(1)

            def kv_wait(h):
                s.w(P, "tensor", "dma", kv_tgt[h])

            ysb_dve = [0, 0]
            ytp_dve = [0, 0]

            def out_cb(h, j, yp, rec_ap, pe_c, d_rec):
                i = (h * TPC + j) % 2
                s.w(P, "vector", "pe", pe_c)
                s.w(P, "vector", "dve", max(d_rec, ytp_dve[i]))
                d_y = s.op(P, "vector", "dve", lambda e, yp=yp, i=i:
                           e.tensor_scalar(ysb[i][:], yp[:], rec_ap, None,
                                           ALU.mult))
                ysb_dve[i] = d_y
                s.w(P, "tensor", "dve", d_y)
                p_t = s.op(P, "tensor", "pe", lambda e, i=i: e.transpose(
                    yt_ps[i][:], ysb[i][:], ident[:]))
                s.w(P, "vector", "pe", p_t)
                ytp_dve[i] = s.op(P, "vector", "dve", lambda e, h=h, j=j, i=i:
                                  e.tensor_copy(
                    aoT[:, h * 1024 + j * 128: h * 1024 + (j + 1) * 128],
                    yt_ps[i][:]))
                if j == TPC - 1:
                    head_pe_done[h] = s.c["pe"]
                    if h + 2 < 16:
                        emit_kv_dma(h + 2)
                return d_y

            em_attn_head_rows(
                nc, P, s, st3, "at", list(range(16)),
                qt_ap=lambda h, j: qt[:, h * 1024 + j * 128: h * 1024 + (j + 1) * 128],
                kT_ap=lambda h: kTb[h % 2],
                v_ap=lambda h, t: vb[h % 2][:, t * 128:(t + 1) * 128],
                msk_sb=msk, scale=1.0 / math.sqrt(128.0), out_cb=out_cb,
                kv_wait=kv_wait)
            ao_dve = s.c["dve"]
            flush(nc, P)

        with ExitStack() as st4:
            P = Phase()
            em_proj(nc, P, s, st4, "po", wo_d, 2048, aoT, ao_dve,
                    ep_copy_bf16(nc, P, s, qt))
            flush(nc, P)
            P = Phase()
            em_fm_to_tm(nc, P, s, st4, "ta", qt, r_sb, 16, s.c["dve"],
                        ident, add_into=True)
            flush(nc, P)

        P = Phase()
        em_store_r(nc, P, s, r_o, r_sb,
                   [("dve", s.c["dve"]), ("act", s.c["act"]), ("pe", s.c["pe"])])
        flush(nc, P)
    nc.finalize()
    return nc


def build_l0attn_kv():
    nc = bass.Bass("TRN2", target_bir_lowering=False)
    r_d = nc.dram_tensor("r", [1024, 2048], F32, kind="ExternalInput")
    wk_d = nc.dram_tensor("wk", [2048, 512], F32, kind="ExternalInput")
    wv_d = nc.dram_tensor("wv", [2048, 512], F32, kind="ExternalInput")
    cs_d = nc.dram_tensor("cosT", [128, 1024], F32, kind="ExternalInput")
    sn_d = nc.dram_tensor("sinT", [128, 1024], F32, kind="ExternalInput")
    id_d = nc.dram_tensor("ident", [128, 128], BF16, kind="ExternalInput")
    k_o = nc.dram_tensor("k_ro", [1024, 512], BF16, kind="ExternalOutput")
    v_o = nc.dram_tensor("v_tm", [1024, 512], BF16, kind="ExternalOutput")

    with ExitStack() as st:
        s = S(nc, st)
        r_sb = st.enter_context(nc.sbuf_tensor("r_sb", [128, 8 * 2048], F32))
        ident = st.enter_context(nc.sbuf_tensor("id_sb", [128, 128], BF16))
        _IDENT[0] = ident
        cosT = st.enter_context(nc.sbuf_tensor("cosT_sb", [128, 1024], F32))
        sinT = st.enter_context(nc.sbuf_tensor("sinT_sb", [128, 1024], F32))
        scr = st.enter_context(nc.sbuf_tensor("scr", [128, 2048], F32))
        fm = st.enter_context(nc.sbuf_tensor("fm", [128, 4 * 1024], BF16))
        tm = st.enter_context(nc.sbuf_tensor("tm", [128, 8 * 512], BF16))

        P = Phase()
        r_rdy = em_load_r(nc, P, s, r_sb, r_d)
        id_rdy = s.op(P, "sync", "dma", lambda e: e.dma_start(ident[:], id_d[:]))
        s.op(P, "sync", "dma", lambda e: e.dma_start(cosT[:], cs_d[:]))
        cs_rdy = s.op(P, "sync", "dma", lambda e: e.dma_start(sinT[:], sn_d[:]))
        cst, _ = em_consts(nc, P, s, st, [1e-5])
        flush(nc, P)

        with ExitStack() as st2:
            xnt = st2.enter_context(nc.sbuf_tensor("xnt", [128, 16 * 1024], BF16))
            with ExitStack() as stln:
                P = Phase()
                s.w(P, "tensor", "dma", id_rdy)
                xnt_dve, _ = em_ln(nc, P, s, stln, "ln", r_sb, xnt, ident,
                                   r_rdy, cst[:, 0:1])
                flush(nc, P)

            for nm, w_d, o_d, rope in (("k", wk_d, k_o, True),
                                       ("v", wv_d, v_o, False)):
                with ExitStack() as st3:
                    P = Phase()
                    if rope:
                        s.w(P, "vector", "dma", cs_rdy)

                        def ep(oc, ps_ap, pe_c):
                            if oc % 2 == 0:
                                return None
                            d = em_rope_pair(nc, P, s, scr, fm, (oc - 1, oc),
                                             em_proj.last_ps[0],
                                             em_proj.last_ps[1],
                                             cosT[:], sinT[:], pe_c)
                            return (d, d)
                        em_proj(nc, P, s, st3, "pj" + nm, w_d, 512, xnt,
                                xnt_dve, ep, oc_grp=2, npsum=2)
                    else:
                        em_proj(nc, P, s, st3, "pj" + nm, w_d, 512, xnt,
                                xnt_dve, ep_copy_bf16(nc, P, s, fm), oc_grp=2)
                    flush(nc, P)
                    P = Phase()
                    d = em_fm_to_tm(nc, P, s, st3, "tt" + nm, fm, tm, 4,
                                    s.c["dve"], ident)
                    s.w(P, "sync", "dve", d)
                    v = s.op(P, "sync", "dma", lambda e, o_d=o_d: e.dma_start(
                        o_d.rearrange("(m p) d -> p m d", p=128),
                        tm.ap().rearrange("p (m d) -> p m d", m=TPC)))
                    s.w(P, "sync", "dma", v)
                    flush(nc, P)
    nc.finalize()
    return nc


def build_l0attn_main(debug=False):
    nc = bass.Bass("TRN2", target_bir_lowering=False)
    r_d = nc.dram_tensor("r", [1024, 2048], F32, kind="ExternalInput")
    wq_d = nc.dram_tensor("wq", [2048, 2048], F32, kind="ExternalInput")
    wo_d = nc.dram_tensor("wo", [2048, 2048], F32, kind="ExternalInput")
    k_d = nc.dram_tensor("k_ro", [4096, 512], BF16, kind="ExternalInput")
    v_d = nc.dram_tensor("v_tm", [4096, 512], BF16, kind="ExternalInput")
    cs_d = nc.dram_tensor("cosT", [128, 1024], F32, kind="ExternalInput")
    sn_d = nc.dram_tensor("sinT", [128, 1024], F32, kind="ExternalInput")
    mk_d = nc.dram_tensor("masksb", [8, 128, 512], BF16, kind="ExternalInput")
    hs_d = nc.dram_tensor("hsb", [128, 8], F32, kind="ExternalInput")
    dr_d = nc.dram_tensor("dirs_rep", [3, 8 * 128], BF16, kind="ExternalInput")
    id_d = nc.dram_tensor("ident", [128, 128], BF16, kind="ExternalInput")
    r_o = nc.dram_tensor("r_out", [1024, 2048], F32, kind="ExternalOutput")
    if debug:
        qt_o = nc.dram_tensor("qt_o", [128, 16 * 1024], BF16, kind="ExternalOutput")
        ao_o = nc.dram_tensor("ao_o", [128, 16 * 1024], BF16, kind="ExternalOutput")

    with ExitStack() as st:
        s = S(nc, st)
        r_sb = st.enter_context(nc.sbuf_tensor("r_sb", [128, 8 * 2048], F32))
        ident = st.enter_context(nc.sbuf_tensor("id_sb", [128, 128], BF16))
        _IDENT[0] = ident
        qt = st.enter_context(nc.sbuf_tensor("qt", [128, 16 * 1024], BF16))
        aoT = st.enter_context(nc.sbuf_tensor("aoT", [128, 16 * 1024], BF16))
        msk = st.enter_context(nc.sbuf_tensor("msk", [128, 8 * 512], BF16))
        hsb = st.enter_context(nc.sbuf_tensor("hsb_sb", [128, 8], F32))
        drp = st.enter_context(nc.sbuf_tensor("drp", [3, 8 * 128], BF16))
        qpc = st.enter_context(nc.sbuf_tensor("qpc", [128, 64], F32))

        P = Phase()
        r_rdy = em_load_r(nc, P, s, r_sb, r_d)
        id_rdy = s.op(P, "sync", "dma", lambda e: e.dma_start(ident[:], id_d[:]))
        s.op(P, "sync", "dma", lambda e: e.dma_start(
            msk.ap().rearrange("p (j k) -> p j k", j=8),
            mk_d.rearrange("j p k -> p j k")))
        s.op(P, "sync", "dma", lambda e: e.dma_start(hsb[:], hs_d[:]))
        c_rdy = s.op(P, "sync", "dma", lambda e: e.dma_start(drp[:], dr_d[:]))
        cst, _ = em_consts(nc, P, s, st, [1e-5])
        flush(nc, P)

        with ExitStack() as st2:
            xnt = st2.enter_context(nc.sbuf_tensor("xnt", [128, 16 * 1024], BF16))
            cosT = st2.enter_context(nc.sbuf_tensor("cosT_sb", [128, 1024], F32))
            sinT = st2.enter_context(nc.sbuf_tensor("sinT_sb", [128, 1024], F32))
            scr = st2.enter_context(nc.sbuf_tensor("scr", [128, 2048], F32))
            with ExitStack() as stln:
                P = Phase()
                s.w(P, "tensor", "dma", id_rdy)
                cs_rdy = s.op(P, "sync", "dma", lambda e: e.dma_start(cosT[:], cs_d[:]))
                cs_rdy = s.op(P, "sync", "dma", lambda e: e.dma_start(sinT[:], sn_d[:]))
                xnt_dve, _ = em_ln(nc, P, s, stln, "ln", r_sb, xnt, ident,
                                   r_rdy, cst[:, 0:1])
                flush(nc, P)
            P = Phase()
            s.w(P, "vector", "dma", cs_rdy)

            def ep(oc, ps_ap, pe_c):
                if oc % 2 == 0:
                    return None
                d = em_rope_pair(nc, P, s, scr, qt, (oc - 1, oc),
                                 em_proj.last_ps[0], em_proj.last_ps[1],
                                 cosT[:], sinT[:], pe_c)
                return (d, d)
            em_proj(nc, P, s, st2, "pq", wq_d, 2048, xnt, xnt_dve, ep,
                    oc_grp=2, npsum=2)
            flush(nc, P)

        with ExitStack() as stq:
            qp_ps = stq.enter_context(nc.psum_tensor("qp_ps", [128, 8], F32))
            P = Phase()
            s.w(P, "tensor", "dma", c_rdy)
            prev_ev = 0
            for j in range(TPC):
                s.w(P, "tensor", "dve", prev_ev)
                for h in range(8):
                    pe_c = s.op(P, "tensor", "pe", lambda e, j=j, h=h:
                                e.matmul(qp_ps[:, h:h + 1],
                                         qt[0:3, (2 * h) * 1024 + j * 128:
                                            (2 * h) * 1024 + (j + 1) * 128],
                                         drp[:, h * 128:h * 128 + 1],
                                         start=True, stop=True))
                s.w(P, "vector", "pe", pe_c)
                prev_ev = s.op(P, "vector", "dve", lambda e, j=j: e.tensor_copy(
                    qpc[:, j * 8:(j + 1) * 8], qp_ps[:]))
            flush(nc, P)

        with ExitStack() as st3:
            kTb = st3.enter_context(nc.sbuf_tensor("kTb", [128, 2 * 4096], BF16))
            vb = st3.enter_context(nc.sbuf_tensor("vb", [128, 32 * 256], BF16))
            kpb = st3.enter_context(nc.sbuf_tensor("kpb", [128, 4096], BF16))
            psb = [st3.enter_context(nc.sbuf_tensor(f"psb{i}", [128, 4096], BF16))
                   for i in range(2)]
            pgb = st3.enter_context(nc.sbuf_tensor("pgb", [128, 4096], BF16))
            ptb = st3.enter_context(nc.sbuf_tensor("ptb", [128, 1024], BF16))
            gsc = st3.enter_context(nc.sbuf_tensor("gsc", [128, 512], F32))
            sumsp = st3.enter_context(nc.sbuf_tensor("sums", [128, 32], F32))
            abr = st3.enter_context(nc.sbuf_tensor("abr", [128, 8], F32))
            s_ps = [st3.enter_context(nc.psum_tensor(f"s_ps{i}", [128, 512], F32))
                    for i in range(2)]
            g_ps = [st3.enter_context(nc.psum_tensor(f"g_ps{i}", [128, 512], F32))
                    for i in range(2)]
            t_ps = [st3.enter_context(nc.psum_tensor(f"t_ps{i}", [128, 512], BF16))
                    for i in range(2)]
            av_ps = st3.enter_context(nc.psum_tensor("av_ps", [128, 512], F32))

            P = Phase()
            for eng in ("tensor", "vector", "scalar"):
                s.w(P, eng, "dma", c_rdy)
            kv_tgt = [0, 0]
            kv_pe_done = [0, 0]
            exp_s_done = [0, 0]
            exp_g_done = [0, 0]
            kp_evac = [0, 0]
            tev = [0, 0]
            av_evac = 0
            prev_row_red = 0

            def emit_kv_dma(kv):
                if kv == 1:
                    s.w(P, "scalar", "pe", kv_pe_done[0])
                    s.w(P, "sync", "pe", kv_pe_done[0])
                t1 = 0
                for hf in range(2):
                    t1 = s.op(P, "scalar", "dma", lambda e, kv=kv, hf=hf:
                              e.dma_start_transpose(
                        kTb[:, hf * 4096:(hf + 1) * 4096],
                        k_d[:, kv * 256 + hf * 128: kv * 256 + (hf + 1) * 128]))
                t2 = s.op(P, "sync", "dma", lambda e, kv=kv: e.dma_start(
                    vb.ap().rearrange("p (t d) -> p t d", t=32),
                    v_d.rearrange("(t p) d -> p t d", p=128)
                       [:, :, kv * 256:(kv + 1) * 256]))
                kv_tgt[kv] = max(t1, t2)

            emit_kv_dma(0)
            for kv in range(2):
                s.w(P, "tensor", "dma", kv_tgt[kv])
                s.w(P, "scalar", "dma", kv_tgt[kv])
                for pair in range(2):
                    hA = 4 * kv + 2 * pair
                    for j in range(TPC):
                        nkb = j + 1
                        for u in range(2):
                            h = hA + u
                            pb = psb[u]
                            for kb in range(nkb):
                                sp = s_ps[kb % 2]
                                s.w(P, "tensor", "act", exp_s_done[kb % 2])
                                if kb == 0:
                                    s.w(P, "scalar", "dve", prev_row_red)
                                for hf in range(2):
                                    pe_c = s.op(P, "tensor", "pe",
                                                lambda e, h=h, j=j, kb=kb, hf=hf, sp=sp:
                                                e.matmul(
                                        sp[:],
                                        qt[:, (2 * h + hf) * 1024 + j * 128:
                                           (2 * h + hf) * 1024 + (j + 1) * 128],
                                        kTb[:, hf * 4096 + kb * 512:
                                            hf * 4096 + (kb + 1) * 512],
                                        start=(hf == 0), stop=(hf == 1)))
                                if kb == j:
                                    s.w(P, "vector", "pe", pe_c)
                                    d_m = s.op(P, "vector", "dve",
                                               lambda e, j=j, sp=sp: e.tensor_tensor(
                                        sp[:], sp[:],
                                        msk[:, j * 512:(j + 1) * 512], ALU.add))
                                    s.w(P, "scalar", "dve", d_m)
                                else:
                                    s.w(P, "scalar", "pe", pe_c)
                                exp_s_done[kb % 2] = s.op(
                                    P, "scalar", "act", lambda e, kb=kb, sp=sp, pb=pb, u=u:
                                    e.activation(pb[:, kb * 512:(kb + 1) * 512],
                                                 sp[:], AF.Exp, scale=1.0 / 16.0,
                                                 accum_out=sumsp[:, u * 8 + kb:u * 8 + kb + 1]))
                            for kb in range(nkb):
                                gp = g_ps[kb % 2]
                                s.w(P, "tensor", "dve", kp_evac[kb % 2])
                                pe_c = s.op(P, "tensor", "pe", lambda e, h=h, kb=kb, gp=gp:
                                            e.matmul(
                                    gp[:], drp[:, h * 128:(h + 1) * 128],
                                    kTb[0:3, kb * 512:(kb + 1) * 512],
                                    start=True, stop=True))
                                s.w(P, "vector", "pe", pe_c)
                                s.w(P, "vector", "act", exp_g_done[kb % 2])
                                kp_evac[kb % 2] = s.op(
                                    P, "vector", "dve", lambda e, kb=kb, gp=gp:
                                    e.tensor_copy(kpb[:, kb * 512:(kb + 1) * 512], gp[:]))
                                if kb == j:
                                    s.selfw(P, "vector", "dve")
                                    s.w(P, "vector", "act",
                                        max(exp_g_done[0], exp_g_done[1]))
                                    d_g = s.op(P, "vector", "dve",
                                               lambda e, j=j, h=h, kb=kb: e.scalar_tensor_tensor(
                                        gsc[:], kpb[:, kb * 512:(kb + 1) * 512],
                                        qpc[:, j * 8 + h:j * 8 + h + 1],
                                        msk[:, j * 512:(j + 1) * 512],
                                        ALU.mult, ALU.add))
                                    s.w(P, "scalar", "dve", d_g)
                                    exp_g_done[kb % 2] = s.op(
                                        P, "scalar", "act", lambda e, kb=kb, pb=pb, u=u:
                                        e.activation(pgb[:, kb * 512:(kb + 1) * 512],
                                                     gsc[:], AF.Exp,
                                                     accum_out=sumsp[:, 16 + u * 8 + kb:16 + u * 8 + kb + 1]))
                                else:
                                    s.w(P, "scalar", "dve", kp_evac[kb % 2])
                                    exp_g_done[kb % 2] = s.op(
                                        P, "scalar", "act", lambda e, j=j, h=h, kb=kb, pb=pb, u=u:
                                        e.activation(pgb[:, kb * 512:(kb + 1) * 512],
                                                     kpb[:, kb * 512:(kb + 1) * 512],
                                                     AF.Exp,
                                                     scale=qpc[:, j * 8 + h:j * 8 + h + 1],
                                                     accum_out=sumsp[:, 16 + u * 8 + kb:16 + u * 8 + kb + 1]))
                            a_exp = s.c["act"]
                            s.w(P, "vector", "act", a_exp)
                            c0 = u * 4
                            s.op(P, "vector", "dve", lambda e, u=u, nkb=nkb, c0=c0:
                                 e.reduce_sum(abr[:, c0:c0 + 1],
                                              sumsp[:, u * 8:u * 8 + nkb], axis=AX))
                            prev_row_red = s.op(
                                P, "vector", "dve", lambda e, u=u, nkb=nkb, c0=c0:
                                e.reduce_sum(abr[:, c0 + 1:c0 + 2],
                                             sumsp[:, 16 + u * 8:16 + u * 8 + nkb], axis=AX))
                            s.selfw(P, "vector", "dve")
                            s.op(P, "vector", "dve", lambda e, c0=c0: e.reciprocal(
                                abr[:, c0:c0 + 1], abr[:, c0:c0 + 1]))
                            s.op(P, "vector", "dve", lambda e, c0=c0: e.reciprocal(
                                abr[:, c0 + 1:c0 + 2], abr[:, c0 + 1:c0 + 2]))
                            s.selfw(P, "vector", "dve")
                            s.op(P, "vector", "dve", lambda e, h=h, c0=c0:
                                 e.tensor_tensor(abr[:, c0 + 2:c0 + 3],
                                                 abr[:, c0:c0 + 1],
                                                 hsb[:, h:h + 1], ALU.mult))
                            s.op(P, "vector", "dve", lambda e, h=h, c0=c0:
                                 e.tensor_tensor(abr[:, c0 + 3:c0 + 4],
                                                 abr[:, c0 + 1:c0 + 2],
                                                 hsb[:, h:h + 1], ALU.mult))
                            s.selfw(P, "vector", "dve")
                            s.op(P, "vector", "dve", lambda e, c0=c0:
                                 e.tensor_tensor(abr[:, c0 + 2:c0 + 3],
                                                 abr[:, c0:c0 + 1],
                                                 abr[:, c0 + 2:c0 + 3], ALU.subtract))
                            s.selfw(P, "vector", "dve")
                            for kb in range(nkb):
                                s.op(P, "vector", "dve", lambda e, kb=kb, pb=pb, c0=c0:
                                     e.tensor_scalar(pb[:, kb * 512:(kb + 1) * 512],
                                                     pb[:, kb * 512:(kb + 1) * 512],
                                                     abr[:, c0 + 2:c0 + 3], None,
                                                     ALU.mult))
                                s.selfw(P, "vector", "dve")
                                s.op(P, "vector", "dve", lambda e, kb=kb, pb=pb, c0=c0:
                                     e.scalar_tensor_tensor(
                                         pb[:, kb * 512:(kb + 1) * 512],
                                         pgb[:, kb * 512:(kb + 1) * 512],
                                         abr[:, c0 + 3:c0 + 4],
                                         pb[:, kb * 512:(kb + 1) * 512],
                                         ALU.mult, ALU.add))
                        d_comb = s.c["dve"]
                        s.w(P, "tensor", "dve", d_comb)
                        first = True
                        for kb in range(nkb):
                            for kc in range(4):
                                t = kb * 4 + kc
                                tp = t_ps[t % 2]
                                s.w(P, "tensor", "dve", tev[t % 2])
                                for u in range(2):
                                    p_t = s.op(P, "tensor", "pe",
                                               lambda e, t=t, u=u, tp=tp: e.transpose(
                                        tp[:, u * 128:(u + 1) * 128],
                                        psb[u][:, t * 128:(t + 1) * 128],
                                        ident[:]))
                                s.w(P, "vector", "pe", p_t)
                                tev[t % 2] = s.op(P, "vector", "dve",
                                                  lambda e, t=t, tp=tp: e.tensor_copy(
                                    ptb[:, (t % 4) * 256:(t % 4) * 256 + 256],
                                    tp[:, 0:256]))
                                s.w(P, "tensor", "dve", tev[t % 2])
                                if first:
                                    s.w(P, "tensor", "dve", av_evac)
                                for hf in range(2):
                                    pe_c = s.op(P, "tensor", "pe",
                                                lambda e, t=t, hf=hf, first=first:
                                                e.matmul(
                                        av_ps[:, hf * 256:(hf + 1) * 256],
                                        vb[:, t * 256 + hf * 128: t * 256 + (hf + 1) * 128],
                                        ptb[:, (t % 4) * 256:(t % 4) * 256 + 256],
                                        start=(first and hf == 0),
                                        stop=(t == nkb * 4 - 1 and hf == 1)))
                                first = False
                        s.w(P, "vector", "pe", pe_c)
                        for u in range(2):
                            for hf in range(2):
                                h = hA + u
                                av_evac = s.op(P, "vector", "dve",
                                               lambda e, u=u, hf=hf, h=h, j=j:
                                               e.tensor_copy(
                                    aoT[:, (2 * h + hf) * 1024 + j * 128:
                                        (2 * h + hf) * 1024 + (j + 1) * 128],
                                    av_ps[:, hf * 256 + u * 128: hf * 256 + u * 128 + 128]))
                    kv_pe_done[kv] = s.c["pe"]
                if kv == 0:
                    emit_kv_dma(1)
            ao_dve = s.c["dve"]
            if debug:
                s.w(P, "sync", "dve", ao_dve)
                dv = s.op(P, "sync", "dma", lambda e: e.dma_start(qt_o[:], qt[:]))
                dv = s.op(P, "sync", "dma", lambda e: e.dma_start(ao_o[:], aoT[:]))
                s.w(P, "sync", "dma", dv)
            flush(nc, P)

        with ExitStack() as st4:
            P = Phase()
            em_proj(nc, P, s, st4, "po", wo_d, 2048, aoT, ao_dve,
                    ep_copy_bf16(nc, P, s, qt))
            flush(nc, P)
            P = Phase()
            em_fm_to_tm(nc, P, s, st4, "ta", qt, r_sb, 16, s.c["dve"],
                        ident, add_into=True)
            flush(nc, P)

        P = Phase()
        em_store_r(nc, P, s, r_o, r_sb,
                   [("dve", s.c["dve"]), ("act", s.c["act"]), ("pe", s.c["pe"])])
        flush(nc, P)
    nc.finalize()
    return nc


def host_rope_tables(c):
    pos = np.concatenate([np.arange(t * 128, (t + 1) * 128) for t in TILES[c]])
    inv = 1.0 / (10000.0 ** (np.arange(0, 256, 2, dtype=np.float32) / 256.0))
    f = pos[None, :].astype(np.float32) * inv[:, None]
    return np.cos(f).astype(np.float32), np.sin(f).astype(np.float32)


def host_dirs_rep():
    trig = np.array([[sx, sy, sz] for sx in (1.0, -1.0) for sy in (1.0, -1.0)
                     for sz in (1.0, -1.0)], dtype=np.float32) / np.sqrt(3.0)
    drp = np.repeat(trig.T, 128, axis=1)
    return np.ascontiguousarray(drp).astype(BFD)


def build_moe():
    nc = bass.Bass("TRN2", target_bir_lowering=False)
    r_d = nc.dram_tensor("r", [1024, 2048], F32, kind="ExternalInput")
    gu_d = nc.dram_tensor("gate_upT", [17, 2048, 1536], F32, kind="ExternalInput")
    dn_d = nc.dram_tensor("downT", [17, 768, 2048], F32, kind="ExternalInput")
    rw_d = nc.dram_tensor("router_wT", [2048, 16], F32, kind="ExternalInput")
    eb_d = nc.dram_tensor("ebb", [128, 16], F32, kind="ExternalInput")
    sl_d = nc.dram_tensor("sel16", [16, 2048], BF16, kind="ExternalInput")
    id_d = nc.dram_tensor("ident", [128, 128], BF16, kind="ExternalInput")
    r_o = nc.dram_tensor("r_out", [1024, 2048], F32, kind="ExternalOutput")

    GRPS = [[0, 1], [2, 3], [4, 5], [6, 7], [8, 9], [10, 11], [12, 13],
            [14, 15], [16]]
    NW = 5

    with ExitStack() as st:
        s = S(nc, st)
        r_sb = st.enter_context(nc.sbuf_tensor("r_sb", [128, 8 * 2048], F32))
        ident = st.enter_context(nc.sbuf_tensor("id_sb", [128, 128], BF16))
        _IDENT[0] = ident
        rwt = st.enter_context(nc.sbuf_tensor("rwt", [128, 16 * 16], BF16))
        ebb = st.enter_context(nc.sbuf_tensor("ebb_sb", [128, 16], F32))
        sel = st.enter_context(nc.sbuf_tensor("sel_sb", [16, 2048], BF16))
        cmbT = st.enter_context(nc.sbuf_tensor("cmbT", [16, 1024], BF16))
        xnt = st.enter_context(nc.sbuf_tensor("xnt", [128, 16 * 1024], BF16))

        P = Phase()
        r_rdy = em_load_r(nc, P, s, r_sb, r_d)
        id_rdy = s.op(P, "sync", "dma", lambda e: e.dma_start(ident[:], id_d[:]))
        s.op(P, "sync", "dma", lambda e: e.dma_start(ebb[:], eb_d[:]))
        cs_rdy = s.op(P, "sync", "dma", lambda e: e.dma_start(sel[:], sl_d[:]))
        rw_rdy = s.op(P, "gpsimd", "wdma", lambda e: e.dma_start(
            rwt.ap().rearrange("p (kc m) -> p kc m", kc=16),
            rw_d.rearrange("(kc p) m -> p kc m", p=128)))
        cst, _ = em_consts(nc, P, s, st, [1e-5, 1e-20])
        flush(nc, P)

        with ExitStack() as stln:
            P = Phase()
            s.w(P, "tensor", "dma", id_rdy)
            xnt_dve, _ = em_ln(nc, P, s, stln, "ln", r_sb, xnt, ident, r_rdy,
                               cst[:, 0:1])
            flush(nc, P)

        with ExitStack() as str_:
            sc = str_.enter_context(nc.sbuf_tensor("sc", [128, 16], F32))
            bs = str_.enter_context(nc.sbuf_tensor("bs", [128, 16], F32))
            t8 = str_.enter_context(nc.sbuf_tensor("t8", [128, 8], F32))
            oh = str_.enter_context(nc.sbuf_tensor("oh", [128, 64], F32))
            w4 = str_.enter_context(nc.sbuf_tensor("w4", [128, 8], F32))
            cmb = str_.enter_context(nc.sbuf_tensor("cmb", [128, 16], F32))
            tmp16 = str_.enter_context(nc.sbuf_tensor("tmp16", [128, 16], F32))
            cmbb = str_.enter_context(nc.sbuf_tensor("cmbb", [128, 16], BF16))
            g_ps = str_.enter_context(nc.psum_tensor("g_ps", [128, 16], F32))
            c_ps = str_.enter_context(nc.psum_tensor("c_ps", [16, 128], BF16))
            P = Phase()
            s.w(P, "tensor", "dve", xnt_dve)
            s.w(P, "tensor", "wdma", rw_rdy)
            s.w(P, "vector", "dma", cs_rdy)
            prev_ev = 0
            prev_act = 0
            for m in range(TPC):
                s.w(P, "tensor", "act", prev_act)
                for kc in range(16):
                    pe_c = s.op(P, "tensor", "pe", lambda e, m=m, kc=kc:
                                e.matmul(g_ps[:],
                                         xnt[:, kc * 1024 + m * 128: kc * 1024 + (m + 1) * 128],
                                         rwt[:, kc * 16:(kc + 1) * 16],
                                         start=(kc == 0), stop=(kc == 15)))
                s.w(P, "scalar", "pe", pe_c)
                s.w(P, "scalar", "dve", prev_ev)
                prev_act = s.op(P, "scalar", "act", lambda e: e.activation(
                    sc[:], g_ps[:], AF.Sigmoid))
                s.w(P, "vector", "act", prev_act)
                s.selfw(P, "vector", "dve")
                s.op(P, "vector", "dve", lambda e: e.tensor_tensor(
                    bs[:], sc[:], ebb[:], ALU.add))
                s.selfw(P, "vector", "dve")
                s.op(P, "vector", "dve", lambda e: e.max(t8[:], bs[:]))
                s.selfw(P, "vector", "dve")
                for jj in range(4):
                    s.op(P, "vector", "dve", lambda e, jj=jj: e.tensor_scalar(
                        oh[:, jj * 16:(jj + 1) * 16], bs[:], t8[:, jj:jj + 1],
                        None, ALU.is_equal))
                    s.selfw(P, "vector", "dve")
                    s.op(P, "vector", "dve", lambda e, jj=jj: e.tensor_tensor(
                        tmp16[:], sc[:], oh[:, jj * 16:(jj + 1) * 16], ALU.mult))
                    s.selfw(P, "vector", "dve")
                    s.op(P, "vector", "dve", lambda e, jj=jj: e.reduce_sum(
                        w4[:, jj:jj + 1], tmp16[:], axis=AX))
                    s.selfw(P, "vector", "dve")
                s.op(P, "vector", "dve", lambda e: e.reduce_sum(
                    w4[:, 4:5], w4[:, 0:4], axis=AX))
                s.selfw(P, "vector", "dve")
                s.op(P, "vector", "dve", lambda e: e.tensor_tensor(
                    w4[:, 5:6], w4[:, 4:5], cst[:, 1:2], ALU.add))
                s.selfw(P, "vector", "dve")
                s.op(P, "vector", "dve", lambda e: e.reciprocal(
                    w4[:, 6:7], w4[:, 5:6]))
                s.selfw(P, "vector", "dve")
                for jj in range(4):
                    s.op(P, "vector", "dve", lambda e, jj=jj: e.tensor_tensor(
                        w4[:, 7:8], w4[:, jj:jj + 1], w4[:, 6:7], ALU.mult))
                    s.selfw(P, "vector", "dve")
                    if jj == 0:
                        s.op(P, "vector", "dve", lambda e, jj=jj: e.tensor_scalar(
                            cmb[:], oh[:, jj * 16:(jj + 1) * 16], w4[:, 7:8],
                            None, ALU.mult))
                    else:
                        s.op(P, "vector", "dve", lambda e, jj=jj:
                             e.scalar_tensor_tensor(
                                 cmb[:], oh[:, jj * 16:(jj + 1) * 16],
                                 w4[:, 7:8], cmb[:], ALU.mult, ALU.add))
                    s.selfw(P, "vector", "dve")
                d_c = s.op(P, "vector", "dve", lambda e: e.tensor_copy(
                    cmbb[:], cmb[:]))
                s.w(P, "tensor", "dve", d_c)
                s.w(P, "tensor", "dve", prev_ev)
                p_t = s.op(P, "tensor", "pe", lambda e: e.transpose(
                    c_ps[:], cmbb[:], ident[:]))
                s.w(P, "vector", "pe", p_t)
                prev_ev = s.op(P, "vector", "dve", lambda e, m=m: e.tensor_copy(
                    cmbT[:, m * 128:(m + 1) * 128], c_ps[:]))
            cmb_dve = s.c["dve"]
            flush(nc, P)

        with ExitStack() as ste:
            wgu = [ste.enter_context(nc.sbuf_tensor(f"wgu{i}", [128, 16 * 256], BF16))
                   for i in range(NW)]
            dwn = [ste.enter_context(nc.sbuf_tensor(f"dwn{i}", [128, 2 * 6 * 512], BF16))
                   for i in range(2)]
            act_b = ste.enter_context(
                nc.sbuf_tensor("act_b", [128, 2 * 2 * 6 * 512], BF16))
            bcb = ste.enter_context(nc.sbuf_tensor("bcb", [128, 2 * 512], BF16))
            silg = [ste.enter_context(nc.sbuf_tensor(f"silg{i}", [128, 512], BF16))
                    for i in range(2)]
            ub = [ste.enter_context(nc.sbuf_tensor(f"ub{i}", [128, 512], BF16))
                  for i in range(2)]
            gu_ps = [ste.enter_context(nc.psum_tensor(f"gu_ps{i}", [128, 512], F32))
                     for i in range(4)]
            bc_ps = ste.enter_context(nc.psum_tensor("bc_ps", [128, 512], F32))
            yt_ps = [ste.enter_context(nc.psum_tensor(f"yt_ps{i}", [128, 512], F32))
                     for i in range(2)]

            P = Phase()
            s.w(P, "tensor", "dve", max(xnt_dve, cmb_dve))
            s.w(P, "tensor", "dma", cs_rdy)
            gur = gu_d.rearrange("e (kc p) m -> e p kc m", p=128)
            dnr = dn_d.rearrange("e (kc p) m -> e p kc m", p=128)

            load_order = []
            for gi, grp in enumerate(GRPS):
                for e in grp:
                    load_order += [(e, 0), (e, 3), (e, 1), (e, 4), (e, 2), (e, 5)]
            slot_of = {}
            load_tgt = {}
            load_free_pe = [0] * len(load_order)
            emitted = [0]

            def emit_next_load():
                k = emitted[0]
                if k >= len(load_order):
                    return
                emitted[0] += 1
                i = k % NW
                if k >= NW:
                    s.w(P, "gpsimd", "pe", load_free_pe[k - NW])
                e, mg = load_order[k]
                tgt = s.op(P, "gpsimd", "wdma", lambda en, e=e, mg=mg, i=i:
                           en.dma_start(
                    wgu[i].ap().rearrange("p (kc m) -> p kc m", kc=16),
                    gur[e, :, :, mg * 256:(mg + 1) * 256]))
                slot_of[(e, mg)] = i
                load_tgt[(e, mg)] = tgt

            for _ in range(NW - 1):
                emit_next_load()

            dn_tgt = [0, 0]
            dn_free_pe = [0, 0]

            def emit_dn_dma(gi, n):
                buf = n % 2
                grp = GRPS[gi]
                s.w(P, "gpsimd", "pe", dn_free_pe[buf])
                tgt = 0
                for ei, e in enumerate(grp):
                    tgt = s.op(P, "gpsimd", "wdma", lambda en, e=e, ei=ei, n=n,
                               buf=buf: en.dma_start(
                        dwn[buf].ap().rearrange("p (ei f) -> p ei f", ei=2)[:, ei],
                        dnr[e, :, :, n * 512:(n + 1) * 512]))
                dn_tgt[buf] = tgt

            gu_free = [0] * 4
            silg_free = [0, 0]
            ub_free = [0, 0]
            bc_dve = 0
            yt_free = [0, 0]
            loadk = 0

            for gi, grp in enumerate(GRPS):
                for ei, e in enumerate(grp):
                    for m5 in range(2):
                        if e == 16:
                            s.selfw(P, "vector", "dve")
                            bc_dve = s.op(P, "vector", "dve", lambda en, m5=m5:
                                          en.memset(
                                bcb[:, m5 * 512:(m5 + 1) * 512], 1.0))
                        else:
                            s.w(P, "tensor", "dve", bc_dve)
                            pe_c = s.op(P, "tensor", "pe", lambda en, e=e, m5=m5:
                                        en.matmul(bc_ps[:],
                                                  sel[:, e * 128:(e + 1) * 128],
                                                  cmbT[:, m5 * 512:(m5 + 1) * 512],
                                                  start=True, stop=True))
                            s.w(P, "vector", "pe", pe_c)
                            s.selfw(P, "vector", "dve")
                            bc_dve = s.op(P, "vector", "dve", lambda en, m5=m5:
                                          en.tensor_copy(
                                bcb[:, m5 * 512:(m5 + 1) * 512], bc_ps[:]))
                    bcb_rdy = bc_dve
                    for p in range(6):
                        ig = slot_of[(e, p // 2)]
                        iu = slot_of[(e, 3 + p // 2)]
                        colg = (p % 2) * 128 + 0
                        colu = (p % 2) * 128
                        s.w(P, "tensor", "wdma",
                            max(load_tgt[(e, p // 2)], load_tgt[(e, 3 + p // 2)]))
                        for m5 in range(2):
                            q = p * 2 + m5
                            bgi, bui = (2 * q) % 4, (2 * q + 1) % 4
                            bg, bu = gu_ps[bgi], gu_ps[bui]
                            pi = q % 2
                            s.w(P, "tensor", "act", gu_free[bgi])
                            s.w(P, "tensor", "dve", gu_free[bui])
                            for kc in range(16):
                                s.op(P, "tensor", "pe", lambda en, ig=ig, kc=kc,
                                     colg=colg, bg=bg, m5=m5: en.matmul(
                                    bg[:],
                                    wgu[ig][:, kc * 256 + colg: kc * 256 + colg + 128],
                                    xnt[:, kc * 1024 + m5 * 512: kc * 1024 + m5 * 512 + 512],
                                    start=(kc == 0), stop=(kc == 15)))
                            g_pe = s.c["pe"]
                            for kc in range(16):
                                s.op(P, "tensor", "pe", lambda en, iu=iu, kc=kc,
                                     colu=colu, bu=bu, m5=m5: en.matmul(
                                    bu[:],
                                    wgu[iu][:, kc * 256 + colu: kc * 256 + colu + 128],
                                    xnt[:, kc * 1024 + m5 * 512: kc * 1024 + m5 * 512 + 512],
                                    start=(kc == 0), stop=(kc == 15)))
                            u_pe = s.c["pe"]
                            s.w(P, "scalar", "pe", g_pe)
                            s.w(P, "scalar", "dve", silg_free[pi])
                            a_si = s.op(P, "scalar", "act", lambda en, bg=bg, pi=pi:
                                        en.activation(silg[pi][:], bg[:], AF.Silu))
                            gu_free[bgi] = a_si
                            s.w(P, "vector", "pe", u_pe)
                            s.w(P, "vector", "dve", max(ub_free[pi], bcb_rdy))
                            s.selfw(P, "vector", "dve")
                            d_ub = s.op(P, "vector", "dve", lambda en, bu=bu,
                                        pi=pi, m5=m5: en.tensor_tensor(
                                ub[pi][:], bu[:],
                                bcb[:, m5 * 512:(m5 + 1) * 512], ALU.mult))
                            gu_free[bui] = d_ub
                            s.w(P, "vector", "act", a_si)
                            s.selfw(P, "vector", "dve")
                            acol = ((ei * 2 + m5) * 6 + p) * 512
                            d_act = s.op(P, "vector", "dve", lambda en, pi=pi,
                                         acol=acol: en.tensor_tensor(
                                act_b[:, acol:acol + 512], ub[pi][:],
                                silg[pi][:], ALU.mult))
                            silg_free[pi] = d_act
                            ub_free[pi] = d_act
                            if m5 == 1 and p % 2 == 1:
                                load_free_pe[loadk + (p // 2) * 2] = u_pe
                                load_free_pe[loadk + (p // 2) * 2 + 1] = u_pe
                                emit_next_load()
                                emit_next_load()
                    loadk += 6
                act_dve = s.c["dve"]
                emit_dn_dma(gi, 0)
                emit_dn_dma(gi, 1)
                s.w(P, "tensor", "dve", act_dve)
                for n in range(4):
                    buf = n % 2
                    s.w(P, "tensor", "wdma", dn_tgt[buf])
                    for mt in range(TPC):
                        yp = yt_ps[mt % 2]
                        s.w(P, "tensor", "dve", yt_free[mt % 2])
                        ne = len(grp)
                        for ei in range(ne):
                            for kc in range(6):
                                acol = ((ei * 2 + mt // 4) * 6 + kc) * 512 + (mt % 4) * 128
                                s.op(P, "tensor", "pe", lambda en, ei=ei, kc=kc,
                                     acol=acol, buf=buf, n=n, yp=yp: en.matmul(
                                    yp[:],
                                    act_b[:, acol:acol + 128],
                                    dwn[buf][:, (ei * 6 + kc) * 512:(ei * 6 + kc + 1) * 512],
                                    start=(ei == 0 and kc == 0),
                                    stop=(ei == ne - 1 and kc == 5)))
                        y_pe = s.c["pe"]
                        s.w(P, "vector", "pe", y_pe)
                        s.selfw(P, "vector", "dve")
                        yt_free[mt % 2] = s.op(P, "vector", "dve",
                                               lambda en, mt=mt, n=n, yp=yp:
                                               e_stt_r(en, r_sb, mt, n, yp))
                    dn_free_pe[buf] = s.c["pe"]
                    if n + 2 < 4:
                        emit_dn_dma(gi, n + 2)
            flush(nc, P)

        P = Phase()
        em_store_r(nc, P, s, r_o, r_sb,
                   [("dve", s.c["dve"]), ("act", s.c["act"]), ("pe", s.c["pe"])])
        flush(nc, P)
    nc.finalize()
    return nc


def e_stt_r(en, r_sb, mt, n, yp):
    sl = r_sb[:, mt * 2048 + n * 512: mt * 2048 + (n + 1) * 512]
    return en.scalar_tensor_tensor(sl, yp[:], 0.5, sl, ALU.mult, ALU.add)
